# version 05 (final)
# baseline (speedup 1.0000x reference)
import numpy as np
import concourse.bass as bass
import concourse.mybir as mybir
from concourse.bass_utils import run_bass_kernel_spmd
from concourse.bass_isa import ReduceOp

F32 = mybir.dt.float32
BF16 = mybir.dt.bfloat16
I32 = mybir.dt.int32
AF = mybir.ActivationFunctionType
OP = mybir.AluOpType
AX = mybir.AxisListType.X

NEGBIG = -1.0e30
P = 128
D = 1024
TOWN = 2048
NT = TOWN // P
NTE = NT + 2
E3 = 3072
I0 = 2816
NI0 = I0 // P
I1 = 4096
NKT = 32
PAY_T2 = TOWN * D
PAY_Q = 8 * P * TOWN
PAY = PAY_T2 + 3 * PAY_Q
OFF_T2, OFF_Q, OFF_K, OFF_V = 0, PAY_T2, PAY_T2 + PAY_Q, PAY_T2 + 2 * PAY_Q

DEBUG = False
FENCE_A = False
FENCE_B = False
FENCE_C = False
FENCE_D = True
ENGS = ("sy", "t", "v", "s", "g")
ALLSEMS = ("LD", "GD", "TT", "VV", "SS", "GG", "BC")
ALIBI_SLOPES = [2.0 ** (-0.5 * (h + 1)) for h in range(16)]


class Builder:
    def __init__(self):
        self.nc = bass.Bass("TRN2", target_bir_lowering=False, debug=False)
        self.q = {e: [] for e in ENGS}
        self.cnt = {s: 0 for s in ALLSEMS}
        self.lastwait = {e: {} for e in ENGS}
        self.fencing = {e: False for e in ENGS}

    def fence(self):
        for e in ENGS:
            self.fencing[e] = True

    def op(self, eng, fn, wait=(), inc=None, inc_by=1):
        wl = list(wait)
        if eng == "v":
            wl.append("VV")
        elif eng == "s":
            wl.append("SS")
        if self.fencing[eng]:
            wl = list(ALLSEMS)
            self.fencing[eng] = False
        w = []
        for s in wl:
            v = self.cnt[s]
            if v > self.lastwait[eng].get(s, -1):
                w.append((s, v))
                self.lastwait[eng][s] = v
        if inc is not None:
            self.cnt[inc] += inc_by
        self.q[eng].append((w, fn, inc, inc_by))

    def dma(self, eng, out, in_, sem, wait=()):
        self.op(eng, lambda E, o=out, i=in_: E.dma_start(out=o, in_=i),
                wait=wait, inc=sem, inc_by=16)

    def finalize(self):
        nc = self.nc
        import contextlib
        with contextlib.ExitStack() as st:
            sems = {n: st.enter_context(nc.semaphore(n)) for n in ALLSEMS}
            block = st.enter_context(nc.Block())

            def runner(engname):
                def run(E):
                    for w, fn, inc, inc_by in self.q[engname]:
                        for s, v in w:
                            E.wait_ge(sems[s], v)
                        ins = fn(E)
                        if ins is not None and inc is not None:
                            ins.then_inc(sems[inc], inc_by)
                return run

            block.sync(runner("sy"))
            block.tensor(runner("t"))
            block.vector(runner("v"))
            block.scalar(runner("s"))
            block.gpsimd(runner("g"))
        return nc


class Arena:
    def __init__(self, ap, ncols):
        self.f32 = ap
        self.ncols = ncols
        self.off = 0
        self.stack = []
        self.peak = 0

    def push(self):
        self.stack.append(self.off)

    def pop(self):
        self.off = self.stack.pop()

    def alloc(self, cols, dtype=F32):
        n = cols if dtype in (F32, I32) else (cols + 1) // 2
        o = self.off
        self.off += n + (-n) % 2
        self.peak = max(self.peak, self.off)
        assert self.off <= self.ncols, f"SBUF arena overflow: {self.off}"
        if dtype == F32:
            return self.f32[:, o:o + cols]
        if dtype == I32:
            return self.f32.bitcast(I32)[:, o:o + cols]
        return self.f32.bitcast(BF16)[:, 2 * o:2 * o + cols]


def build_program():
    b = Builder()
    nc = b.nc

    def din(name, shape):
        return nc.dram_tensor(name, list(shape), F32, kind="ExternalInput").ap()

    x_ext = din("x_ext", (NTE * P, D))
    w_in = din("L0_attn_w_in", (E3, D))
    b_in = din("L0_attn_b_in", (1, E3))
    w_out0 = din("L0_attn_w_out", (D, D))
    b_out0 = din("L0_attn_b_out", (1, D))
    clip_x = din("L0_ffn_clip_x", (1, 1))
    clip_dn = din("L0_ffn_clip_down", (1, 1))
    w_gate = din("L0_ffn_w_gate", (I0, D))
    w_up = din("L0_ffn_w_up", (I0, D))
    w_down = din("L0_ffn_w_down", (D, I0))
    w_qkv1 = din("L1_attn_W_qkv", (D, E3))
    w_out1 = din("L1_attn_W_out", (D, D))
    w1_d = din("L1_ffn_W1", (I1, D))
    b1_d = din("L1_ffn_b1", (1, I1))
    w2_d = din("L1_ffn_W2", (D, I1))
    b2_d = din("L1_ffn_b2", (1, D))
    ident_d = din("ident", (P, P))
    mleft_d = din("mleft", (P, P))
    mright_d = din("mright", (P, P))
    edgeL_d = din("edgeL", (P, P))
    edgeR_d = din("edgeR", (P, P))
    m0_d = din("m0", (P, P))
    m1_d = din("m1", (P, P))
    dtab_d = din("dtab", (P, 8 * 512))
    sel_d = din("sel", (P, 2))

    out_d = nc.dram_tensor("out", [TOWN, D], F32, kind="ExternalOutput").ap()
    if DEBUG:
        dbg_t1 = nc.dram_tensor("dbg_t1", [TOWN, D], F32,
                                kind="ExternalOutput").ap()
        dbg_t2 = nc.dram_tensor("dbg_t2", [TOWN, D], F32,
                                kind="ExternalOutput").ap()
        dbg_t3 = nc.dram_tensor("dbg_t3", [TOWN, D], F32,
                                kind="ExternalOutput").ap()
        dbg_sc = nc.dram_tensor("dbg_sc", [P, 8], F32,
                                kind="ExternalOutput").ap()
        dbg_sc2 = nc.dram_tensor("dbg_sc2", [P, 8], F32,
                                 kind="ExternalOutput").ap()
        dbg_u = nc.dram_tensor("dbg_u", [P, 32], F32,
                               kind="ExternalOutput").ap()
        dbg_qkv = nc.dram_tensor("dbg_qkv", [3 * P, 512], F32,
                                 kind="ExternalOutput").ap()
        dbg_px = nc.dram_tensor("dbg_px", [P, 1024], F32,
                                kind="ExternalOutput").ap()
        dbg_tt = nc.dram_tensor("dbg_tt", [P, 1024], F32,
                                kind="ExternalOutput").ap()
        dbg_cj = nc.dram_tensor("dbg_cj", [P, 1024], F32,
                                kind="ExternalOutput").ap()
        dbg_cm = nc.dram_tensor("dbg_cm", [P, P], F32,
                                kind="ExternalOutput").ap()
        dbg_csp = nc.dram_tensor("dbg_csp", [P, TOWN], BF16,
                                 kind="ExternalOutput").ap()
        dbg_csl = nc.dram_tensor("dbg_csl", [P, TOWN], BF16,
                                 kind="ExternalOutput").ap()
        dbg_ln = nc.dram_tensor("dbg_ln", [P, 16], F32,
                                kind="ExternalOutput").ap()
        dbg_xq = nc.dram_tensor("dbg_xq", [P, D], BF16,
                                kind="ExternalOutput").ap()
        dbg_g = nc.dram_tensor("dbg_g", [P, 512], F32,
                               kind="ExternalOutput").ap()
        dbg_hh = nc.dram_tensor("dbg_hh", [P, 512], F32,
                                kind="ExternalOutput").ap()
        dbg_dt = nc.dram_tensor("dbg_dt", [P, D], F32,
                                kind="ExternalOutput").ap()
        dbg_wg = nc.dram_tensor("dbg_wg", [P, D], BF16,
                                kind="ExternalOutput").ap()
        dbg_pc = nc.dram_tensor("dbg_pc", [P, 256], F32,
                                kind="ExternalOutput").ap()
        dbg_pe = nc.dram_tensor("dbg_pe", [P, 1024], F32,
                                kind="ExternalOutput").ap()
        dbg_ps = nc.dram_tensor("dbg_ps", [P, 1024], F32,
                                kind="ExternalOutput").ap()
        dbg_q2 = nc.dram_tensor("dbg_q2", [P, NTE * P + 16], F32,
                                kind="ExternalOutput").ap()
        dbg_pc2 = nc.dram_tensor("dbg_pc2", [P, 256], F32,
                                 kind="ExternalOutput").ap()
        dbg_kf = nc.dram_tensor("dbg_kf", [P, NKT * P], F32,
                                kind="ExternalOutput").ap()
        dbg_qm = nc.dram_tensor("dbg_qm", [P, TOWN], F32,
                                kind="ExternalOutput").ap()
        dbg_l = nc.dram_tensor("dbg_l", [P, 8 * NT * 2], F32,
                               kind="ExternalOutput").ap()
        dbg_cxh = nc.dram_tensor("dbg_cxh", [8, P, TOWN], BF16,
                                 kind="ExternalOutput").ap()
        dbg_cxl = nc.dram_tensor("dbg_cxl", [8, P, TOWN], BF16,
                                 kind="ExternalOutput").ap()


    t1_sp = nc.dram_tensor("t1_sp", [NT, P, D], F32, kind="Internal").ap()
    t3_sp = nc.dram_tensor("t3_sp", [NT, P, D], F32, kind="Internal").ap()
    h_sp = nc.dram_tensor("h_sp", [NI0, P, TOWN], F32, kind="Internal").ap()
    w1q_sp = nc.dram_tensor("w1q_sp", [8, P, I1], BF16, kind="Internal").ap()
    w2q_sp = nc.dram_tensor("w2q_sp", [32, P, D], BF16, kind="Internal").ap()
    wd_sp = nc.dram_tensor("wd_sp", [8, NI0, P, P], BF16, kind="Internal").ap()
    d_sp = nc.dram_tensor("d_sp", [NT, P, D], F32, kind="Internal").ap()
    ctx0h_sp = nc.dram_tensor("ctx0h_sp", [8, P, TOWN], BF16, kind="Internal").ap()
    ctx0l_sp = nc.dram_tensor("ctx0l_sp", [8, P, TOWN], BF16, kind="Internal").ap()
    ctx1h_sp = nc.dram_tensor("ctx1h_sp", [8, P, TOWN], BF16, kind="Internal").ap()
    ctx1l_sp = nc.dram_tensor("ctx1l_sp", [8, P, TOWN], BF16, kind="Internal").ap()
    ar1_in = nc.dram_tensor("ar1_in", [P, 1], F32, kind="Internal").ap()
    ar1_out = nc.dram_tensor("ar1_out", [P, 1], F32, kind="Internal").ap()
    ar2_in = nc.dram_tensor("ar2_in", [P, 1], F32, kind="Internal").ap()
    ar2_out = nc.dram_tensor("ar2_out", [P, 1], F32, kind="Internal").ap()
    ag_in = nc.dram_tensor("ag_in", [1, PAY], F32, kind="Internal").ap()
    ag_out = nc.dram_tensor("ag_out", [2, PAY], F32, kind="Internal").ap()

    def pay2d(off, rows, cols):
        return ag_in[0:1, off:off + rows * cols].rearrange(
            "a (p m) -> (a p) m", p=rows)

    def pout2d(r, off, rows, cols):
        return ag_out[r:r + 1, off:off + rows * cols].rearrange(
            "a (p m) -> (a p) m", p=rows)

    pbk = [nc.alloc_psum_tensor(f"pb{i}", [P, 512], F32).ap() for i in range(4)]
    pfl = nc.alloc_psum_tensor("pfl", [1, 8], F32).ap()
    pc = nc.alloc_psum_tensor("pc", [P, 512], F32).ap()
    pt = nc.alloc_psum_tensor("pt", [P, 128], F32).ap()
    ptb = nc.alloc_psum_tensor("ptb", [P, 1024], BF16).ap()

    ARN = 45000
    big = nc.alloc_sbuf_tensor("big", [P, ARN], F32).ap()
    ar = Arena(big, ARN)

    ident = ar.alloc(P)
    identb = ar.alloc(P, BF16)
    mleft = ar.alloc(P)
    mright = ar.alloc(P)
    edgeL = ar.alloc(P)
    edgeR = ar.alloc(P)
    m0t = ar.alloc(P)
    m1t = ar.alloc(P)
    sel = ar.alloc(2)
    onesrow = ar.alloc(P)
    onesrow_b = ar.alloc(P, BF16)
    hcol = ar.alloc(2)
    hcolb = ar.alloc(2, BF16)
    scr = ar.alloc(64)
    cell = ar.alloc(32)
    nU0_0 = ar.alloc(16)
    nU0_1 = ar.alloc(16)
    LNS = ar.alloc(16)
    sq_scr = ar.alloc(D)
    gcol = ar.alloc(8)
    dbgc = ar.alloc(8)
    negb16 = ar.alloc(1)
    dlybuf = ar.alloc(512)
    cellb = ar.alloc(4, BF16)
    epscol = ar.alloc(1)
    R1C, S1C, R2C, S2C = 0, 1, 2, 3
    W1SC, W2SC = 4, 6

    for src, dst in ((ident_d, ident), (mleft_d, mleft), (mright_d, mright),
                     (edgeL_d, edgeL), (edgeR_d, edgeR), (m0_d, m0t),
                     (m1_d, m1t), (sel_d, sel)):
        b.dma("sy", dst, src, "LD")
    b.op("v", lambda E: E.memset(onesrow[0:1, :], 1.0), inc="VV")
    b.op("v", lambda E: E.memset(epscol, 1e-5), inc="VV")
    b.op("v", lambda E: E.memset(onesrow_b[0:1, :], 1.0), inc="VV")
    b.op("v", lambda E: E.memset(hcol, 0.0), inc="VV")
    b.op("v", lambda E: E.memset(hcol[0:64, 0:1], 1.0), inc="VV")
    b.op("v", lambda E: E.memset(hcol[64:128, 1:2], 1.0), inc="VV")
    b.op("v", lambda E: E.memset(hcolb, 0.0), inc="VV")
    b.op("v", lambda E: E.memset(dlybuf, 0.0), inc="VV")
    b.op("v", lambda E: E.memset(negb16, -16.0), inc="VV")
    b.op("v", lambda E: E.memset(hcolb[0:64, 0:1], 1.0), inc="VV")
    b.op("v", lambda E: E.memset(hcolb[64:128, 1:2], 1.0), inc="VV")
    b.op("v", lambda E: E.tensor_copy(identb, ident), wait=["LD"], inc="VV")
    selA, selB = sel[:, 0:1], sel[:, 1:2]

    def pe_flush():
        pass

    def vdelay(wait=("TT",)):
        b.op("v", lambda E: E.tensor_copy(dlybuf[:, 256:512],
                                          dlybuf[:, 0:256]),
             wait=list(wait), inc="VV")
        b.op("v", lambda E: E.tensor_copy(dlybuf[:, 0:256],
                                          dlybuf[:, 256:512]), inc="VV")

    def ln_apply(dst, srctile, dst_wait=()):
        s1 = LNS[:, 0:1]; s2 = LNS[:, 1:2]; mcol = LNS[:, 2:3]
        e2 = LNS[:, 3:4]; var = LNS[:, 4:5]; rstd = LNS[:, 5:6]
        nb = LNS[:, 6:7]; msq = LNS[:, 7:8]
        b.op("v", lambda E: E.reduce_sum(s1, srctile, axis=AX), inc="VV")
        b.op("s", lambda E: E.square(sq_scr, srctile), wait=["VV"], inc="SS")
        b.op("v", lambda E: E.reduce_sum(s2, sq_scr, axis=AX), wait=["SS"],
             inc="VV")
        b.op("v", lambda E: E.tensor_scalar(mcol, s1, 1.0 / D, None, OP.mult),
             inc="VV")
        b.op("v", lambda E: E.tensor_scalar(e2, s2, 1.0 / D, None, OP.mult),
             inc="VV")
        b.op("v", lambda E: E.tensor_tensor(msq, mcol, mcol, OP.mult),
             inc="VV")
        b.op("v", lambda E: E.tensor_tensor(var, e2, msq, OP.subtract),
             inc="VV")
        b.op("s", lambda E: E.activation(msq, var, AF.Sqrt, bias=epscol),
             wait=["VV"], inc="SS")
        b.op("v", lambda E: E.reciprocal(rstd, msq), wait=["SS"], inc="VV")
        b.op("v", lambda E: E.scalar_tensor_tensor(
            nb, mcol, -1.0, rstd, OP.mult, OP.mult), inc="VV")
        b.op("s", lambda E: E.activation(dst, srctile, AF.Identity, bias=nb,
                                         scale=rstd),
             wait=["VV"] + list(dst_wait), inc="SS")

    def bcast_cell(dstcol, srccell):
        b.dma("sy", dstcol[0:1, :], srccell, "BC", wait=["VV", "SS"])
        n = 1
        while n < P:
            b.dma("sy", dstcol[n:2 * n, :], dstcol[0:n, :], "BC",
                  wait=["BC"])
            n *= 2

    def col_reduce(col, alu, dstcell):
        b.op("v", lambda E: E.memset(sq_scr[:, 0:P], 0.0), wait=["SS", "TT"],
             inc="VV")
        b.op("v", lambda E: E.tensor_copy(sq_scr[:, 0:1], col), inc="VV")
        b.op("t", lambda E: E.transpose(pt, sq_scr[:, 0:P], ident),
             wait=["VV"], inc="TT")
        vdelay()
        b.op("v", lambda E, a_=alu: E.tensor_reduce(
            dstcell, pt[0:1, :], AX, a_), wait=["TT"], inc="VV")

    def headmax_sq(src, width, dstcell_base, scratch):
        b.op("s", lambda E, w_=width: E.square(scratch[:, 0:w_], src),
             wait=["VV", "TT"], inc="SS")
        for hh in range(2):
            nch = (width + 511) // 512
            for ci in range(nch):
                a, z = ci * 512, min((ci + 1) * 512, width)
                b.op("t", lambda E, h_=hh, a_=a, z_=z: E.matmul(
                    pbk[3][0:1, 0:z_ - a_], hcolb[:, h_:h_ + 1],
                    scratch[:, a_:z_], start=True, stop=True),
                    wait=["SS", "VV"], inc="TT")
                vdelay()
                b.op("v", lambda E, a_=a, z_=z: E.reduce_max(
                    cell[0:1, 30:31], pbk[3][0:1, 0:z_ - a_], axis=AX),
                    wait=["TT"], inc="VV")
                if ci == 0:
                    b.op("v", lambda E, h_=hh: E.tensor_copy(
                        cell[0:1, dstcell_base + h_:dstcell_base + h_ + 1],
                        cell[0:1, 30:31]), inc="VV")
                else:
                    b.op("v", lambda E, h_=hh: E.tensor_tensor(
                        cell[0:1, dstcell_base + h_:dstcell_base + h_ + 1],
                        cell[0:1, dstcell_base + h_:dstcell_base + h_ + 1],
                        cell[0:1, 30:31], OP.max), inc="VV")

    def neg_u0(qcell, kcell, dstcol):
        b.op("s", lambda E: E.sqrt(cell[0:1, 28:29], qcell), wait=["VV"],
             inc="SS")
        b.op("s", lambda E: E.sqrt(cell[0:1, 29:30], kcell), inc="SS")
        b.op("v", lambda E: E.tensor_tensor(
            cell[0:1, 28:29], cell[0:1, 28:29], cell[0:1, 29:30], OP.mult),
            wait=["SS"], inc="VV")
        b.op("v", lambda E: E.tensor_scalar(
            cell[0:1, 28:29], cell[0:1, 28:29], -0.12503, None, OP.mult),
            inc="VV")
        b.op("v", lambda E: E.tensor_scalar(
            cell[0:1, 28:29], cell[0:1, 28:29], 60.0, -8.0,
            OP.add, OP.min), inc="VV")
        bcast_cell(dstcol, cell[0:1, 28:29])

    ar.push()
    xT = ar.alloc(8 * NTE * P)
    xstg = ar.alloc(2 * D)
    wslab = ar.alloc(3 * D)
    wstgA = ar.alloc(D)
    bslab = ar.alloc(4)
    qT0 = ar.alloc(NTE * P)
    kT0 = ar.alloc(NTE * P)
    vT0 = ar.alloc(NTE * P)
    vaug0 = ar.alloc(NTE * 130)
    pexp0 = ar.alloc(2 * 512)
    ctmp0 = ar.alloc(P)
    ctxsp0 = ar.alloc(2 * P, BF16)
    nrmscr0 = ar.alloc(NTE * P, BF16)

    def xTs(dc):
        return xT[:, dc * NTE * P:(dc + 1) * NTE * P]

    for tt in range(NTE):
        stg = xstg[:, (tt % 2) * D:(tt % 2 + 1) * D]
        b.dma("sy", stg, x_ext[tt * P:(tt + 1) * P, :], "LD", wait=["TT"])
        for dc in range(8):
            b.op("t", lambda E, s_=stg, d=dc: E.transpose(
                pt, s_[:, d * P:(d + 1) * P], ident), wait=["LD", "VV"],
                inc="TT")
            b.op("v", lambda E, d=dc, t_=tt: E.tensor_copy(
                xTs(d)[:, t_ * P:(t_ + 1) * P], pt), wait=["TT"], inc="VV")

    for hp in range(8):
        if FENCE_A:
            b.fence()
        for pi, rowbase in enumerate((hp * P, D + hp * P, 2 * D + hp * P)):
            b.dma("sy", wstgA, w_in[rowbase:rowbase + P, :], "LD",
                  wait=["TT", "VV"])
            for dc in range(8):
                b.op("t", lambda E, d=dc: E.transpose(
                    pt, wstgA[:, d * P:(d + 1) * P], ident),
                    wait=["LD", "VV"], inc="TT")
                b.op("v", lambda E, p_=pi, d=dc: E.tensor_copy(
                    wslab[:, p_ * D + d * P:p_ * D + (d + 1) * P], pt),
                    wait=["TT"], inc="VV")
            b.dma("sy", bslab[:, pi:pi + 1],
                  b_in[0:1, rowbase:rowbase + P].rearrange(
                      "a (p m) -> (a p) m", p=P), "LD", wait=["SS"])
        for pi, dst in enumerate((qT0, kT0, vT0)):
            for ncb in range(5):
                a, z = ncb * 512, min((ncb + 1) * 512, NTE * P)
                bank = pbk[ncb % 2]
                for dc in range(8):
                    b.op("t", lambda E, p_=pi, d=dc, a_=a, z_=z, bk=bank:
                         E.matmul(bk[:, 0:z_ - a_],
                                  wslab[:, p_ * D + d * P:p_ * D + (d + 1) * P],
                                  xTs(d)[:, a_:z_],
                                  start=(d == 0), stop=(d == 7)),
                         wait=["LD", "SS", "VV"], inc="TT")
                pe_flush()
                b.op("s", lambda E, p_=pi, ds=dst, a_=a, z_=z, bk=bank:
                     E.activation(ds[:, a_:z_], bk[:, 0:z_ - a_], AF.Identity,
                                  bias=bslab[:, p_:p_ + 1]),
                     wait=["TT", "LD"], inc="SS")
        b.op("v", lambda E: E.memset(vaug0, 1.0), wait=["TT"], inc="VV")
        for tt in range(NTE):
            b.op("t", lambda E, t_=tt: E.transpose(
                pt, vT0[:, t_ * P:(t_ + 1) * P], ident),
                wait=["SS", "VV"], inc="TT")
            b.op("v", lambda E, t_=tt: E.tensor_copy(
                vaug0[:, t_ * 130:(t_ + 1) * 130].rearrange(
                    "p (a m) -> p a m", a=2)[:, :, 0:64],
                pt.rearrange("p (a m) -> p a m", a=2)), wait=["TT"], inc="VV")
        if DEBUG and hp == 0:
            b.op("v", lambda E: E.tensor_copy(dbgc[:, 6:7],
                                              nU0_0[:, 0:1]),
                 wait=["BC"], inc="VV")
            b.dma("g", dbg_q2[:, 0:NTE * P], qT0, "GD",
                  wait=["SS", "VV", "TT"])
        if DEBUG and hp == 0:
            b.dma("g", dbg_qkv[0:P, :], qT0[:, 0:512], "GD",
                  wait=["SS", "VV"])
            b.dma("g", dbg_qkv[P:2 * P, :], kT0[:, 0:512], "GD")
            b.dma("g", dbg_qkv[2 * P:3 * P, :], vT0[:, 0:512], "GD")
        for j in range(NT):
            if FENCE_A:
                b.fence()
            for hh in range(2):
                base = hh * 64
                sbank = pbk[hh]
                for m in range(3):
                    b.op("t", lambda E, b_=base, j_=j, m_=m, bk=sbank:
                         E.matmul(bk[:, m_ * P:(m_ + 1) * P],
                                  kT0[b_:b_ + 64,
                                      (j_ + m_) * P:(j_ + m_ + 1) * P],
                                  qT0[b_:b_ + 64, (j_ + 1) * P:(j_ + 2) * P],
                                  start=True, stop=True),
                         wait=["SS", "VV"], inc="TT")
                mk0 = edgeL if j == 0 else mleft
                mk2 = edgeR if j == NT - 1 else mright
                pe_flush()
                b.op("v", lambda E, bk=sbank, mk=mk0: E.tensor_tensor(
                    bk[:, 0:P], bk[:, 0:P], mk, OP.add), wait=["TT"],
                    inc="VV")
                b.op("v", lambda E, bk=sbank, mk=mk2: E.tensor_tensor(
                    bk[:, 2 * P:3 * P], bk[:, 2 * P:3 * P], mk, OP.add),
                    inc="VV")
                b.op("s", lambda E, bk=sbank, h_=hh: E.activation(
                    pexp0[:, h_ * 512:h_ * 512 + 384], bk[:, 0:384], AF.Exp,
                    scale=0.125, bias=negb16), wait=["VV", "TT"], inc="SS")
                for m in range(3):
                    b.op("t", lambda E, h_=hh, m_=m, j_=j: E.matmul(
                        pc[:, h_ * P:h_ * P + 65],
                        pexp0[:, h_ * 512 + m_ * P:h_ * 512 + (m_ + 1) * P],
                        vaug0[:, (j_ + m_) * 130 + h_ * 65:
                              (j_ + m_) * 130 + (h_ + 1) * 65],
                        start=(m_ == 0), stop=(m_ == 2)),
                        wait=["SS"], inc="TT")
            if DEBUG and hp == 0 and j == 0:
                b.op("v", lambda E: E.tensor_copy(nrmscr0[:, 0:256],
                                                  pc[:, 0:256]),
                     wait=["TT"], inc="VV")
                b.dma("g", dbg_pc, nrmscr0[:, 0:256], "GD", wait=["VV"])
                b.dma("g", dbg_pe[:, 0:512], pexp0[:, 0:512], "GD",
                      wait=["SS"])
                b.dma("g", dbg_pe[:, 512:1024], pexp0[:, 512:1024], "GD")
                b.op("v", lambda E: E.tensor_copy(nrmscr0[:, 256:256 + 384],
                                                  pbk[0][:, 0:384]),
                     inc="VV")
                b.op("v", lambda E: E.tensor_copy(nrmscr0[:, 640:640 + 384],
                                                  pbk[1][:, 0:384]),
                     inc="VV")
                b.dma("g", dbg_ps[:, 0:384], nrmscr0[:, 256:640], "GD",
                      wait=["VV"])
                b.dma("g", dbg_ps[:, 512:896], nrmscr0[:, 640:1024], "GD")
            for hh in range(2):
                b.op("v", lambda E, h_=hh: E.reciprocal(
                    scr[:, 20 + h_:21 + h_], pc[:, h_ * P + 64:h_ * P + 65]),
                    wait=["TT"], inc="VV")
                b.op("s", lambda E, h_=hh: E.activation(
                    ctmp0[:, h_ * 64:(h_ + 1) * 64],
                    pc[:, h_ * P:h_ * P + 64], AF.Identity,
                    scale=scr[:, 20 + h_:21 + h_]), wait=["VV"], inc="SS")
            if DEBUG and hp == 0 and j == 0:
                b.dma("g", dbg_cm, ctmp0, "GD", wait=["SS"])
            b.op("t", lambda E: E.transpose(pt, ctmp0, ident),
                 wait=["SS", "VV", "GD"], inc="TT")
            b.op("s", lambda E: E.activation(
                ctxsp0[:, 0:P], pt, AF.Identity), wait=["TT", "GD"],
                inc="SS")
            b.op("v", lambda E: E.tensor_tensor(
                ctxsp0[:, P:2 * P], pt, ctxsp0[:, 0:P], OP.subtract),
                wait=["SS"], inc="VV")
            b.dma("g", ctx0h_sp[hp][:, j * P:(j + 1) * P], ctxsp0[:, 0:P],
                  "GD", wait=["VV"])
            b.dma("g", ctx0l_sp[hp][:, j * P:(j + 1) * P],
                  ctxsp0[:, P:2 * P], "GD")
    ar.pop()
    b.fence()

    ar.push()
    wo_hi = ar.alloc(8 * D, BF16)
    wo_lo = ar.alloc(8 * D, BF16)
    wstg3 = ar.alloc(D)
    bo_hi = ar.alloc(D, BF16)
    bo_lo = ar.alloc(D, BF16)
    bo_f = ar.alloc(D)
    xtile = ar.alloc(D)
    ttile = ar.alloc(D)
    t1tile = ar.alloc(D)
    amax1 = ar.alloc(1)
    cjhi = ar.alloc(8 * P, BF16)
    cjlo = ar.alloc(8 * P, BF16)

    for ec in range(8):
        b.dma("sy", wstg3, w_out0[ec * P:(ec + 1) * P, :], "LD",
              wait=["TT", "VV", "SS"])
        for dc in range(8):
            b.op("t", lambda E, d=dc: E.transpose(
                pt, wstg3[:, d * P:(d + 1) * P], ident),
                wait=["LD", "VV", "SS"], inc="TT")
            b.op("s", lambda E, d=dc, e_=ec: E.activation(
                wo_hi[:, d * D + e_ * P:d * D + (e_ + 1) * P], pt,
                AF.Identity), wait=["TT"], inc="SS")
            b.op("v", lambda E, d=dc, e_=ec: E.tensor_tensor(
                wo_lo[:, d * D + e_ * P:d * D + (e_ + 1) * P], pt,
                wo_hi[:, d * D + e_ * P:d * D + (e_ + 1) * P], OP.subtract),
                wait=["SS"], inc="VV")
    b.dma("sy", bo_f[0:1, :], b_out0, "LD")
    b.op("s", lambda E: E.activation(bo_hi[0:1, :], bo_f[0:1, :],
                                     AF.Identity), wait=["LD"], inc="SS")
    b.op("v", lambda E: E.tensor_tensor(bo_lo[0:1, :], bo_f[0:1, :],
                                        bo_hi[0:1, :], OP.subtract),
         wait=["SS"], inc="VV")
    b.op("v", lambda E: E.memset(amax1, 0.0), inc="VV")

    for j in range(NT):
        if FENCE_A:
            b.fence()
        b.dma("sy", xtile, x_ext[(j + 1) * P:(j + 2) * P, :], "LD",
              wait=["VV"])
        for dc in range(8):
            b.dma("sy", cjhi[:, dc * P:(dc + 1) * P],
                  ctx0h_sp[dc][:, j * P:(j + 1) * P], "LD", wait=["TT"])
            b.dma("sy", cjlo[:, dc * P:(dc + 1) * P],
                  ctx0l_sp[dc][:, j * P:(j + 1) * P], "LD")
        for ec in range(2):
            bank = pbk[ec]
            mms = []
            for L, R in ((cjhi, wo_hi), (cjhi, wo_lo), (cjlo, wo_hi)):
                for dc in range(8):
                    mms.append((L, R, dc))
            for idx, (L, R, dc) in enumerate(mms):
                b.op("t", lambda E, L_=L, R_=R, d=dc, e_=ec, bk=bank,
                     s_=(idx == 0):
                     E.matmul(bk, L_[:, d * P:(d + 1) * P],
                              R_[:, d * D + e_ * 512:d * D + (e_ + 1) * 512],
                              start=s_, stop=False),
                     wait=["VV", "SS", "LD"], inc="TT")
            b.op("t", lambda E, e_=ec, bk=bank: E.matmul(
                bk, onesrow_b[0:1, :], bo_hi[0:1, e_ * 512:(e_ + 1) * 512],
                start=False, stop=False), inc="TT")
            b.op("t", lambda E, e_=ec, bk=bank: E.matmul(
                bk, onesrow_b[0:1, :], bo_lo[0:1, e_ * 512:(e_ + 1) * 512],
                start=False, stop=True), inc="TT")
            pe_flush()
            b.op("v", lambda E, e_=ec, bk=bank: E.scalar_tensor_tensor(
                ttile[:, e_ * 512:(e_ + 1) * 512],
                xtile[:, e_ * 512:(e_ + 1) * 512], 2.0, bk,
                OP.mult, OP.add), wait=["TT", "LD"], inc="VV")
        if DEBUG and j == 0:
            b.dma("g", dbg_tt, ttile, "GD", wait=["VV"])
            b.op("v", lambda E: E.tensor_copy(sq_scr, cjhi), wait=["LD"],
                 inc="VV")
            b.dma("g", dbg_cj, sq_scr, "GD", wait=["VV"])
        ln_apply(t1tile, ttile, dst_wait=("GD",))
        if DEBUG and j == 0:
            b.dma("g", dbg_ln, LNS, "GD", wait=["VV", "SS"])
        b.dma("g", t1_sp[j], t1tile, "GD", wait=["SS"])
        if DEBUG:
            b.dma("g", dbg_t1[j * P:(j + 1) * P, :], t1tile, "GD")
        b.op("v", lambda E: E.reduce_max(scr[:, 22:23], t1tile, axis=AX,
                                         apply_absolute_value=True),
             wait=["SS"], inc="VV")
        b.op("v", lambda E: E.tensor_tensor(amax1, amax1, scr[:, 22:23],
                                            OP.max), inc="VV")
    b.op("v", lambda E: E.tensor_copy(dbgc[:, 0:1], amax1), inc="VV")
    col_reduce(amax1, OP.max, cell[0:1, 4:5])
    bcast_cell(amax1, cell[0:1, 4:5])
    b.op("v", lambda E: E.tensor_copy(dbgc[:, 1:2], amax1), inc="VV")
    b.dma("g", ar1_in, amax1, "GD", wait=["VV", "BC"])
    b.op("g", lambda E: E.collective_compute(
        "AllReduce", OP.max, replica_groups=[list(range(8))],
        ins=[ar1_in.opt()], outs=[ar1_out.opt()]), wait=["GD"], inc="GG")
    b.dma("g", scr[:, 24:25], ar1_out, "GD", wait=["GG"])
    b.dma("sy", cell[0:1, 0:1], clip_x, "LD")
    b.op("s", lambda E: E.activation(cell[0:1, 1:2], cell[0:1, 0:1],
                                     AF.Sigmoid), wait=["LD"], inc="SS")
    bcast_cell(scr[:, 25:26], cell[0:1, 1:2])
    b.op("v", lambda E: E.tensor_copy(dbgc[:, 2:3], scr[:, 24:25]),
         wait=["GD"], inc="VV")
    b.op("v", lambda E: E.tensor_copy(dbgc[:, 3:4], scr[:, 25:26]), inc="VV")
    b.op("v", lambda E: E.tensor_tensor(scr[:, 26:27], scr[:, 24:25],
                                        scr[:, 25:26], OP.mult),
         wait=["GD", "BC", "VV"], inc="VV")
    b.op("v", lambda E: E.tensor_scalar(gcol[:, S1C:S1C + 1], scr[:, 26:27],
                                        1.0 / 127.0, None, OP.mult),
         inc="VV")
    b.op("v", lambda E: E.reciprocal(gcol[:, R1C:R1C + 1],
                                     gcol[:, S1C:S1C + 1]), inc="VV")
    ar.pop()
    b.fence()

    ar.push()
    t2T = ar.alloc(8 * TOWN)
    ttileB = ar.alloc(D)
    i32t = ar.alloc(D, I32)
    xqbf = ar.alloc(D, BF16)
    wstgB = ar.alloc(I0)
    wqiB = ar.alloc(I0, BF16)
    amax2 = ar.alloc(1)
    gsc = ar.alloc(4)
    wdsc = ar.alloc(8)
    t2tile = ar.alloc(D)
    ar.push()
    xqT = ar.alloc(8 * TOWN, BF16)
    wgT = ar.alloc(D, BF16)
    wuT = ar.alloc(D, BF16)
    gtile = ar.alloc(TOWN)
    util = ar.alloc(TOWN)
    htile = ar.alloc(TOWN)

    for j in range(NT):
        if FENCE_B:
            b.fence()
        b.dma("sy", ttileB, t1_sp[j], "LD", wait=["TT", "VV", "SS"])
        b.op("v", lambda E: E.tensor_scalar(
            sq_scr, ttileB, gcol[:, R1C:R1C + 1], None, OP.mult),
            wait=["LD", "SS"], inc="VV")
        b.op("v", lambda E: E.tensor_copy(i32t, sq_scr), inc="VV")
        b.op("v", lambda E: E.tensor_copy(xqbf, i32t), inc="VV")
        b.op("v", lambda E: E.tensor_scalar(xqbf, xqbf, -128.0, 127.0,
                                            OP.max, OP.min), inc="VV")
        if DEBUG and j == 0:
            b.dma("g", dbg_xq, xqbf, "GD", wait=["VV"])
        for dc in range(8):
            b.op("t", lambda E, d=dc: E.transpose(
                ptb[:, d * P:(d + 1) * P], xqbf[:, d * P:(d + 1) * P],
                identb), wait=["VV"], inc="TT")
        b.op("v", lambda E, j_=j: E.tensor_copy(
            xqT.rearrange("p (a m) -> p a m", a=8)[:, :, j_ * P:(j_ + 1) * P],
            ptb.rearrange("p (a m) -> p a m", a=8)), wait=["TT"], inc="VV")

    b.op("v", lambda E: E.memset(amax2, 0.0), inc="VV")
    for it in range(NI0):
        if FENCE_B:
            b.fence()
        for wi, (wd_, wT_) in enumerate(((w_gate, wgT), (w_up, wuT))):
            b.dma("sy", wstgB[:, 0:D], wd_[it * P:(it + 1) * P, :], "LD",
                  wait=["TT", "VV"])
            b.op("v", lambda E: E.reduce_max(
                scr[:, 27:28], wstgB[:, 0:D], axis=AX,
                apply_absolute_value=True), wait=["LD"], inc="VV")
            b.op("v", lambda E: E.tensor_scalar(
                scr[:, 28:29], scr[:, 27:28], 1.0 / 127.0, None, OP.mult),
                inc="VV")
            b.op("v", lambda E: E.reciprocal(scr[:, 29:30], scr[:, 28:29]),
                 inc="VV")
            b.op("v", lambda E, w_=wi: E.tensor_tensor(
                gsc[:, w_:w_ + 1], scr[:, 28:29], gcol[:, S1C:S1C + 1],
                OP.mult), inc="VV")
            b.op("v", lambda E: E.tensor_scalar(
                sq_scr, wstgB[:, 0:D], scr[:, 29:30], None, OP.mult),
                inc="VV")
            b.op("v", lambda E: E.tensor_copy(i32t, sq_scr), inc="VV")
            b.op("v", lambda E: E.tensor_copy(xqbf, i32t), inc="VV")
            for dc in range(8):
                b.op("t", lambda E, d=dc: E.transpose(
                    ptb[:, d * P:(d + 1) * P], xqbf[:, d * P:(d + 1) * P],
                    identb), wait=["VV"], inc="TT")
            b.op("v", lambda E, w_=wT_: E.tensor_copy(w_, ptb),
                 wait=["TT"], inc="VV")
        for tc in range(4):
            for wi, wT_ in ((0, wgT), (1, wuT)):
                bank = pbk[wi]
                for dc in range(8):
                    b.op("t", lambda E, w_=wT_, d=dc, tc_=tc, bk=bank:
                         E.matmul(bk, w_[:, d * P:(d + 1) * P],
                                  xqT[:, d * TOWN + tc_ * 512:
                                      d * TOWN + (tc_ + 1) * 512],
                                  start=(d == 0), stop=(d == 7)),
                         wait=["VV", "SS"], inc="TT")
            pe_flush()
            b.op("s", lambda E, tc_=tc: E.activation(
                gtile[:, tc_ * 512:(tc_ + 1) * 512], pbk[0], AF.Silu,
                scale=gsc[:, 0:1]), wait=["TT", "VV"], inc="SS")
            b.op("s", lambda E, tc_=tc: E.activation(
                util[:, tc_ * 512:(tc_ + 1) * 512], pbk[1], AF.Identity,
                scale=gsc[:, 1:2]), inc="SS")
        b.op("v", lambda E: E.tensor_tensor(htile, gtile, util, OP.mult),
             wait=["SS", "GD"], inc="VV")
        if DEBUG and it == 0:
            b.dma("g", dbg_g, gtile[:, 0:512], "GD", wait=["VV"])
            b.dma("g", dbg_hh, htile[:, 0:512], "GD")
            b.dma("g", dbg_wg, wgT, "GD")
        b.dma("g", h_sp[it], htile, "GD", wait=["VV"])
        b.op("v", lambda E: E.reduce_max(scr[:, 30:31], htile, axis=AX,
                                         apply_absolute_value=True),
             inc="VV")
        b.op("v", lambda E: E.tensor_tensor(amax2, amax2, scr[:, 30:31],
                                            OP.max), inc="VV")
    col_reduce(amax2, OP.max, cell[0:1, 4:5])
    bcast_cell(amax2, cell[0:1, 4:5])
    b.dma("g", ar2_in, amax2, "GD", wait=["VV", "BC"])
    b.op("g", lambda E: E.collective_compute(
        "AllReduce", OP.max, replica_groups=[list(range(8))],
        ins=[ar2_in.opt()], outs=[ar2_out.opt()]), wait=["GD"], inc="GG")
    b.dma("g", scr[:, 24:25], ar2_out, "GD", wait=["GG"])
    b.dma("sy", cell[0:1, 2:3], clip_dn, "LD")
    b.op("s", lambda E: E.activation(cell[0:1, 3:4], cell[0:1, 2:3],
                                     AF.Sigmoid), wait=["LD"], inc="SS")
    bcast_cell(scr[:, 25:26], cell[0:1, 3:4])
    b.op("v", lambda E: E.tensor_tensor(scr[:, 26:27], scr[:, 24:25],
                                        scr[:, 25:26], OP.mult),
         wait=["GD", "BC", "VV"], inc="VV")
    b.op("v", lambda E: E.tensor_scalar(gcol[:, S2C:S2C + 1], scr[:, 26:27],
                                        1.0 / 127.0, None, OP.mult),
         inc="VV")
    b.op("v", lambda E: E.reciprocal(gcol[:, R2C:R2C + 1],
                                     gcol[:, S2C:S2C + 1]), inc="VV")
    ar.pop()
    b.fence()
    ar.push()
    wdstrm = ar.alloc(2 * NI0 * P, BF16)
    hqs = ar.alloc(NI0 * 512, BF16)
    hstg = ar.alloc(512)
    i32s = ar.alloc(512, I32)
    dTt = ar.alloc(512)
    wtbB = ar.alloc(P, BF16)
    dstage = ar.alloc(4 * D)

    for hc in range(8):
        if FENCE_B:
            b.fence()
        b.dma("sy", wstgB, w_down[hc * P:(hc + 1) * P, :], "LD",
              wait=["TT", "VV"])
        b.op("v", lambda E: E.reduce_max(scr[:, 27:28], wstgB, axis=AX,
                                         apply_absolute_value=True),
             wait=["LD"], inc="VV")
        b.op("v", lambda E: E.tensor_scalar(scr[:, 28:29], scr[:, 27:28],
                                            1.0 / 127.0, None, OP.mult),
             inc="VV")
        b.op("v", lambda E: E.reciprocal(scr[:, 29:30], scr[:, 28:29]),
             inc="VV")
        b.op("v", lambda E, h_=hc: E.tensor_tensor(
            wdsc[:, h_:h_ + 1], scr[:, 28:29], gcol[:, S2C:S2C + 1],
            OP.mult), inc="VV")
        for ch in range(3):
            a, z = ch * D, min((ch + 1) * D, I0)
            b.op("v", lambda E, a_=a, z_=z: E.tensor_scalar(
                sq_scr[:, 0:z_ - a_], wstgB[:, a_:z_], scr[:, 29:30], None,
                OP.mult), inc="VV")
            b.op("v", lambda E, a_=a, z_=z: E.tensor_copy(
                i32t[:, 0:z_ - a_], sq_scr[:, 0:z_ - a_]), inc="VV")
            b.op("v", lambda E, a_=a, z_=z: E.tensor_copy(
                wqiB[:, a_:z_], i32t[:, 0:z_ - a_]), inc="VV")
        for it in range(NI0):
            b.op("t", lambda E, i_=it: E.transpose(
                ptb[:, 0:P], wqiB[:, i_ * P:(i_ + 1) * P], identb),
                wait=["VV", "GD"], inc="TT")
            b.op("v", lambda E: E.tensor_copy(wtbB, ptb[:, 0:P]),
                 wait=["TT", "GD"], inc="VV")
            b.dma("g", wd_sp[hc][it], wtbB, "GD", wait=["VV"])
    for tc in range(4):
        if FENCE_B:
            b.fence()
        for it in range(NI0):
            b.dma("sy", hstg, h_sp[it][:, tc * 512:(tc + 1) * 512], "LD",
                  wait=["TT", "VV"])
            b.op("v", lambda E: E.tensor_scalar(
                sq_scr[:, 0:512], hstg, gcol[:, R2C:R2C + 1], None, OP.mult),
                wait=["LD"], inc="VV")
            b.op("v", lambda E: E.tensor_copy(i32s, sq_scr[:, 0:512]),
                 inc="VV")
            b.op("v", lambda E, i_=it: E.tensor_copy(
                hqs[:, i_ * 512:(i_ + 1) * 512], i32s), inc="VV")
            b.op("v", lambda E, i_=it: E.tensor_scalar(
                hqs[:, i_ * 512:(i_ + 1) * 512],
                hqs[:, i_ * 512:(i_ + 1) * 512], -128.0, 127.0,
                OP.max, OP.min), inc="VV")
        for hc in range(8):
            buf = wdstrm[:, (hc % 2) * NI0 * P:((hc % 2) + 1) * NI0 * P]
            for it0 in range(NI0):
                b.dma("sy", buf[:, it0 * P:(it0 + 1) * P], wd_sp[hc][it0],
                      "LD", wait=["TT", "GD"] if it0 == 0 else [])
            bank = pbk[hc % 2]
            for it in range(NI0):
                b.op("t", lambda E, i_=it, bf=buf, bk=bank: E.matmul(
                    bk, bf[:, i_ * P:(i_ + 1) * P],
                    hqs[:, i_ * 512:(i_ + 1) * 512],
                    start=(i_ == 0), stop=(i_ == NI0 - 1)),
                    wait=["VV", "SS", "LD"], inc="TT")
            pe_flush()
            b.op("s", lambda E, h_=hc, bk=bank: E.activation(
                dTt, bk, AF.Identity, scale=wdsc[:, h_:h_ + 1]),
                wait=["TT"], inc="SS")
            for q4 in range(4):
                b.op("t", lambda E, q_=q4: E.transpose(
                    pt, dTt[:, q_ * P:(q_ + 1) * P], ident),
                    wait=["SS", "VV"], inc="TT")
                b.op("v", lambda E, q_=q4, tc_=tc, h_=hc: E.tensor_copy(
                    sq_scr[:, h_ * P:(h_ + 1) * P] if False else
                    dstage[:, q_ * D + h_ * P:q_ * D + (h_ + 1) * P], pt),
                    wait=["TT", "GD"], inc="VV")
        for q4 in range(4):
            b.dma("g", d_sp[tc * 4 + q4],
                  dstage[:, q4 * D:(q4 + 1) * D], "GD", wait=["VV"])
        if DEBUG and tc == 0:
            b.dma("g", dbg_dt, dstage[:, 0:D], "GD")
    for j in range(NT):
        if FENCE_B:
            b.fence()
        b.dma("sy", ttileB, t1_sp[j], "LD", wait=["TT", "VV", "SS"])
        b.dma("sy", dstage[:, 0:D], d_sp[j], "LD", wait=["GD"])
        b.op("v", lambda E: E.tensor_tensor(
            ttileB, ttileB, dstage[:, 0:D], OP.add), wait=["LD"], inc="VV")
        ln_apply(t2tile, ttileB, dst_wait=("GD",))
        b.dma("g", pay2d(OFF_T2, TOWN, D)[j * P:(j + 1) * P, :],
              t2tile, "GD", wait=["SS"])
        if DEBUG:
            b.dma("g", dbg_t2[j * P:(j + 1) * P, :], t2tile, "GD")
        for dc in range(8):
            b.op("t", lambda E, d=dc: E.transpose(
                pt, t2tile[:, d * P:(d + 1) * P], ident),
                wait=["SS", "VV"], inc="TT")
            b.op("v", lambda E, d=dc, j_=j: E.tensor_copy(
                t2T[:, d * TOWN + j_ * P:d * TOWN + (j_ + 1) * P], pt),
                wait=["TT"], inc="VV")
    ar.pop()
    b.fence()

    ar.push()
    wqs = ar.alloc(3 * 8 * P)
    stg2 = ar.alloc(TOWN)
    vstg = ar.alloc(P)
    for hp in range(8):
        if FENCE_C:
            b.fence()
        for pi in range(3):
            for dc in range(8):
                b.dma("sy", wqs[:, (pi * 8 + dc) * P:(pi * 8 + dc + 1) * P],
                      w_qkv1[dc * P:(dc + 1) * P,
                             pi * D + hp * P:pi * D + (hp + 1) * P], "LD",
                      wait=["TT"])
        for pi, off in ((0, OFF_Q), (1, OFF_K)):
            for ncb in range(4):
                bank = pbk[ncb % 2]
                for dc in range(8):
                    b.op("t", lambda E, p_=pi, d=dc, n_=ncb, bk=bank:
                         E.matmul(bk,
                                  wqs[:, (p_ * 8 + d) * P:
                                      (p_ * 8 + d + 1) * P],
                                  t2T[:, d * TOWN + n_ * 512:
                                      d * TOWN + (n_ + 1) * 512],
                                  start=(d == 0), stop=(d == 7)),
                         wait=["LD", "VV", "GD"], inc="TT")
                pe_flush()
                b.op("v", lambda E, n_=ncb, bk=bank: E.tensor_copy(
                    stg2[:, n_ * 512:(n_ + 1) * 512], bk),
                    wait=["TT", "GD"], inc="VV")
            b.dma("g", pay2d(off + hp * P * TOWN, P, TOWN), stg2, "GD",
                  wait=["VV"])
        for tt in range(NT):
            for dc in range(8):
                b.op("t", lambda E, d=dc, t_=tt: E.matmul(
                    pt, t2T[:, d * TOWN + t_ * P:d * TOWN + (t_ + 1) * P],
                    wqs[:, (2 * 8 + d) * P:(2 * 8 + d + 1) * P],
                    start=(d == 0), stop=(d == 7)),
                    wait=["LD", "VV", "GD"], inc="TT")
            pe_flush()
            b.op("v", lambda E: E.tensor_copy(vstg, pt), wait=["TT", "GD"],
                 inc="VV")
            b.dma("g", pay2d(OFF_V + hp * TOWN * P, TOWN, P)[
                tt * P:(tt + 1) * P, :], vstg, "GD", wait=["VV"])
    pairs = [[0, 1], [2, 3], [4, 5], [6, 7]]
    b.op("g", lambda E: E.collective_compute(
        "AllGather", OP.bypass, replica_groups=pairs,
        ins=[ag_in.opt()], outs=[ag_out.opt()]), wait=["GD"], inc="GG")

    wstgC = ar.alloc(I1)
    wqiC = ar.alloc(I1, BF16)
    i32C = ar.alloc(D, I32)
    wtb = ar.alloc(1024, BF16)
    wsum = ar.alloc(1)

    def tern(w_d, ntile, wcols, sc_idx, spill_fn):
        b.op("v", lambda E: E.memset(wsum, 0.0), inc="VV")
        for wt in range(ntile):
            b.dma("sy", wstgC[:, 0:wcols], w_d[wt * P:(wt + 1) * P, :], "LD",
                  wait=["TT", "VV"])
            b.op("v", lambda E, wc=wcols: E.reduce_sum(
                scr[:, 27:28], wstgC[:, 0:wc], axis=AX,
                apply_absolute_value=True), wait=["LD"], inc="VV")
            b.op("v", lambda E: E.tensor_tensor(wsum, wsum, scr[:, 27:28],
                                                OP.add), inc="VV")
        col_reduce(wsum, OP.add, cell[0:1, 4:5])
        bcast_cell(wsum, cell[0:1, 4:5])
        b.op("v", lambda E, d_=4 if sc_idx == W1SC else 5: E.tensor_copy(
            dbgc[:, d_:d_ + 1], wsum), inc="VV")
        b.op("v", lambda E: E.tensor_scalar(
            gcol[:, sc_idx:sc_idx + 1], wsum, 1.0 / (I1 * D), 1e-5,
            OP.mult, OP.max), wait=["BC"], inc="VV")
        b.op("v", lambda E: E.reciprocal(gcol[:, sc_idx + 1:sc_idx + 2],
                                         gcol[:, sc_idx:sc_idx + 1]),
             inc="VV")
        for wt in range(ntile):
            b.dma("sy", wstgC[:, 0:wcols], w_d[wt * P:(wt + 1) * P, :], "LD",
                  wait=["TT", "VV"])
            for ch in range((wcols + D - 1) // D):
                a, z = ch * D, min((ch + 1) * D, wcols)
                b.op("v", lambda E, a_=a, z_=z: E.tensor_scalar(
                    sq_scr[:, 0:z_ - a_], wstgC[:, a_:z_],
                    gcol[:, sc_idx + 1:sc_idx + 2], None, OP.mult),
                    wait=["LD"], inc="VV")
                b.op("v", lambda E, a_=a, z_=z: E.tensor_copy(
                    i32C[:, 0:z_ - a_], sq_scr[:, 0:z_ - a_]), inc="VV")
                b.op("v", lambda E, a_=a, z_=z: E.tensor_copy(
                    wqiC[:, a_:z_], i32C[:, 0:z_ - a_]), inc="VV")
                b.op("v", lambda E, a_=a, z_=z: E.tensor_scalar(
                    wqiC[:, a_:z_], wqiC[:, a_:z_], -1.0, 1.0,
                    OP.max, OP.min), inc="VV")
            nblk = wcols // P
            for bk8 in range((nblk + 7) // 8):
                nk = min(8, nblk - bk8 * 8)
                for k in range(nk):
                    blk = bk8 * 8 + k
                    b.op("t", lambda E, b_=blk, k_=k: E.transpose(
                        ptb[:, k_ * P:(k_ + 1) * P],
                        wqiC[:, b_ * P:(b_ + 1) * P], identb),
                        wait=["VV"], inc="TT")
                b.op("v", lambda E, n_=nk: E.tensor_copy(
                    wtb[:, 0:n_ * P], ptb[:, 0:n_ * P]),
                    wait=["TT", "GD"], inc="VV")
                for k in range(nk):
                    blk = bk8 * 8 + k
                    spill_fn(wt, blk, wtb[:, k * P:(k + 1) * P])

    def w1_spill(wt, blk, src):
        b.dma("g", w1q_sp[blk][:, wt * P:(wt + 1) * P], src, "GD",
              wait=["VV"])

    def w2_spill(wt, blk, src):
        b.dma("g", w2q_sp[blk][:, wt * P:(wt + 1) * P], src, "GD",
              wait=["VV"])

    tern(w1_d, 32, D, W1SC, w1_spill)
    tern(w2_d, 8, I1, W2SC, w2_spill)
    ar.pop()
    ar.pop()
    b.fence()

    ar.push()
    dtabs = ar.alloc(8 * 512)
    kfull = ar.alloc(NKT * P)
    vaug1 = ar.alloc(NKT * 130)
    qsl = ar.alloc(2 * TOWN)
    qmine = ar.alloc(TOWN)
    pexp1 = ar.alloc(2 * NKT * P)
    ctmp1 = ar.alloc(P)
    ctxsp1 = ar.alloc(2 * P, BF16)
    nrmscr1 = ar.alloc(NKT * P, BF16)

    b.dma("sy", dtabs, dtab_d, "LD", wait=["GG"])
    for hp in range(8):
        if FENCE_D:
            b.fence()
        for r in range(2):
            b.dma("sy", kfull[:, r * TOWN:(r + 1) * TOWN],
                  pout2d(r, OFF_K + hp * P * TOWN, P, TOWN), "LD",
                  wait=["TT", "VV", "GG"])
        b.op("v", lambda E: E.memset(vaug1, 1.0), wait=["TT"], inc="VV")
        for r in range(2):
            for ktl in range(NT):
                kt = r * NT + ktl
                b.dma("sy", vaug1[:, kt * 130:(kt + 1) * 130].rearrange(
                    "p (a m) -> p a m", a=2)[:, :, 0:64],
                    pout2d(r, OFF_V + hp * TOWN * P, TOWN, P)[
                        ktl * P:(ktl + 1) * P, :].rearrange(
                            "p (a m) -> p a m", a=2), "LD", wait=["VV"])
        for r in range(2):
            b.dma("sy", qsl[:, r * TOWN:(r + 1) * TOWN],
                  pout2d(r, OFF_Q + hp * P * TOWN, P, TOWN), "LD",
                  wait=["TT", "VV"])
        for j in range(NT):
            sl = (j // 8) * TOWN
            lt = 2 * (j % 8)
            b.op("v", lambda E, j_=j, s_=sl, l_=lt: E.tensor_scalar(
                qmine[:, j_ * P:(j_ + 1) * P],
                qsl[:, s_ + l_ * P:s_ + (l_ + 1) * P], selA, None, OP.mult),
                wait=["LD"], inc="VV")
            b.op("v", lambda E, j_=j, s_=sl, l_=lt: E.scalar_tensor_tensor(
                qmine[:, j_ * P:(j_ + 1) * P],
                qsl[:, s_ + (l_ + 1) * P:s_ + (l_ + 2) * P], selB,
                qmine[:, j_ * P:(j_ + 1) * P], OP.mult, OP.add), inc="VV")
        if DEBUG and hp == 0:
            b.op("v", lambda E: E.tensor_copy(dbgc[:, 7:8],
                                              nU0_1[:, 0:1]),
                 wait=["BC"], inc="VV")
        for j in range(NT):
            if FENCE_D:
                b.fence()
            cntk = NKT - 2 * j
            ngrp = (cntk + 3) // 4
            for g in range(ngrp):
                cg = min(4, cntk - 4 * g)
                for hh in range(2):
                    base = hh * 64
                    bank = pbk[2 * (g % 2) + hh]
                    for m in range(cg):
                        kt = 2 * j + 4 * g + m
                        b.op("t", lambda E, b_=base, k_=kt, j_=j, m_=m,
                             bk=bank: E.matmul(
                                 bk[:, m_ * P:(m_ + 1) * P],
                                 kfull[b_:b_ + 64, k_ * P:(k_ + 1) * P],
                                 qmine[b_:b_ + 64, j_ * P:(j_ + 1) * P],
                                 start=True, stop=True),
                             wait=["SS", "VV"], inc="TT")
                    h = 2 * hp + hh
                    pe_flush()
                    b.op("v", lambda E, g_=g, c_=cg, bk=bank, h_=h:
                         E.scalar_tensor_tensor(
                             bk[:, 0:c_ * P],
                             dtabs[:, g_ * 512:g_ * 512 + c_ * P],
                             -8.0 * ALIBI_SLOPES[h_], bk[:, 0:c_ * P],
                             OP.mult, OP.add), wait=["TT"], inc="VV")
                    if g == 0:
                        b.op("v", lambda E, bk=bank: E.tensor_tensor(
                            bk[:, 0:P], bk[:, 0:P], m0t, OP.add), inc="VV")
                        b.op("v", lambda E, bk=bank: E.tensor_tensor(
                            bk[:, P:2 * P], bk[:, P:2 * P], m1t, OP.add),
                            inc="VV")
                    b.op("s", lambda E, g_=g, c_=cg, bk=bank, h_=hh:
                         E.activation(
                             pexp1[:, h_ * NKT * P + g_ * 512:
                                   h_ * NKT * P + g_ * 512 + c_ * P],
                             bk[:, 0:c_ * P], AF.Exp, scale=0.125,
                             bias=negb16), wait=["VV", "TT"], inc="SS")
            for hh in range(2):
                for mi in range(cntk):
                    kt = 2 * j + mi
                    b.op("t", lambda E, h_=hh, m_=mi, k_=kt, c_=cntk:
                         E.matmul(
                             pc[:, h_ * P:h_ * P + 65],
                             pexp1[:, h_ * NKT * P + m_ * P:
                                   h_ * NKT * P + (m_ + 1) * P],
                             vaug1[:, k_ * 130 + h_ * 65:
                                   k_ * 130 + (h_ + 1) * 65],
                             start=(m_ == 0), stop=(m_ == c_ - 1)),
                         wait=["SS"], inc="TT")
            pe_flush()
            if DEBUG and hp == 0 and j == 0:
                b.op("v", lambda E: E.tensor_copy(nrmscr1[:, 0:256],
                                                  pc[:, 0:256]),
                     wait=["TT"], inc="VV")
                b.dma("g", dbg_pc2, nrmscr1[:, 0:256], "GD", wait=["VV"])
                b.dma("g", dbg_px[:, 0:512], pexp1[:, 0:512], "GD",
                      wait=["SS"])
                b.dma("g", dbg_px[:, 512:1024], pexp1[:, 512:1024], "GD")
                b.dma("g", dbg_kf[:, 0:2048], kfull[:, 0:2048], "GD")
                b.dma("g", dbg_kf[:, 2048:4096], kfull[:, 2048:4096], "GD")
                b.dma("g", dbg_qm, qmine, "GD")
            vdelay()
            for hh in range(2):
                b.op("v", lambda E, h_=hh: E.reciprocal(
                    scr[:, 20 + h_:21 + h_], pc[:, h_ * P + 64:h_ * P + 65]),
                    wait=["TT"], inc="VV")
                b.op("s", lambda E, h_=hh: E.activation(
                    ctmp1[:, h_ * 64:(h_ + 1) * 64],
                    pc[:, h_ * P:h_ * P + 64], AF.Identity,
                    scale=scr[:, 20 + h_:21 + h_]), wait=["VV"], inc="SS")
            if DEBUG:
                for hh in range(2):
                    b.op("v", lambda E, h_=hh: E.tensor_copy(
                        scr[:, 40 + h_:41 + h_],
                        pc[:, h_ * P + 64:h_ * P + 65]), wait=["TT"],
                        inc="VV")
                b.dma("g", dbg_l[:, (hp * NT + j) * 2:(hp * NT + j) * 2 + 2],
                      scr[:, 40:42], "GD", wait=["VV"])
            b.op("t", lambda E: E.transpose(pt, ctmp1, ident),
                 wait=["SS", "VV"], inc="TT")
            b.op("s", lambda E: E.activation(
                ctxsp1[:, 0:P], pt, AF.Identity), wait=["TT", "GD"],
                inc="SS")
            b.op("v", lambda E: E.tensor_tensor(
                ctxsp1[:, P:2 * P], pt, ctxsp1[:, 0:P], OP.subtract),
                wait=["SS"], inc="VV")
            b.dma("g", ctx1h_sp[hp][:, j * P:(j + 1) * P], ctxsp1[:, 0:P],
                  "GD", wait=["VV"])
            b.dma("g", ctx1l_sp[hp][:, j * P:(j + 1) * P],
                  ctxsp1[:, P:2 * P], "GD")
    ar.pop()
    b.fence()

    ar.push()
    wo1hi = ar.alloc(8 * D, BF16)
    wo1lo = ar.alloc(8 * D, BF16)
    wstgD = ar.alloc(D)
    tmine = ar.alloc(D)
    tstg = ar.alloc(2 * D)
    ttileD = ar.alloc(D)
    t3tile = ar.alloc(D)
    cjhi1 = ar.alloc(8 * P, BF16)
    cjlo1 = ar.alloc(8 * P, BF16)
    for dc in range(8):
        b.dma("sy", wstgD, w_out1[dc * P:(dc + 1) * P, :], "LD",
              wait=["TT", "VV", "SS"])
        b.op("s", lambda E, d=dc: E.activation(
            wo1hi[:, d * D:(d + 1) * D], wstgD, AF.Identity), wait=["LD"],
            inc="SS")
        b.op("v", lambda E, d=dc: E.tensor_tensor(
            wo1lo[:, d * D:(d + 1) * D], wstgD,
            wo1hi[:, d * D:(d + 1) * D], OP.subtract), wait=["SS"],
            inc="VV")
    for j in range(NT):
        if FENCE_D:
            b.fence()
        sl = j // 8
        lt = 2 * (j % 8)
        for r in range(2):
            b.dma("sy", tstg[:, r * D:(r + 1) * D],
                  pout2d(sl, OFF_T2, TOWN, D)[
                      (lt + r) * P:(lt + r + 1) * P, :], "LD",
                  wait=["VV"])
        b.op("v", lambda E: E.tensor_scalar(tmine, tstg[:, 0:D], selA, None,
                                            OP.mult), wait=["LD"], inc="VV")
        b.op("v", lambda E: E.scalar_tensor_tensor(
            tmine, tstg[:, D:2 * D], selB, tmine, OP.mult, OP.add),
            inc="VV")
        for dc in range(8):
            b.dma("sy", cjhi1[:, dc * P:(dc + 1) * P],
                  ctx1h_sp[dc][:, j * P:(j + 1) * P], "LD", wait=["TT"])
            b.dma("sy", cjlo1[:, dc * P:(dc + 1) * P],
                  ctx1l_sp[dc][:, j * P:(j + 1) * P], "LD")
        for ec in range(2):
            bank = pbk[ec]
            mms = []
            for L, R in ((cjhi1, wo1hi), (cjhi1, wo1lo), (cjlo1, wo1hi)):
                for dc in range(8):
                    mms.append((L, R, dc))
            for idx, (L, R, dc) in enumerate(mms):
                b.op("t", lambda E, L_=L, R_=R, d=dc, e_=ec, bk=bank,
                     s_=(idx == 0), z_=(idx == len(mms) - 1):
                     E.matmul(bk, L_[:, d * P:(d + 1) * P],
                              R_[:, d * D + e_ * 512:d * D + (e_ + 1) * 512],
                              start=s_, stop=z_),
                     wait=["VV", "SS", "LD"], inc="TT")
            pe_flush()
            b.op("v", lambda E, e_=ec, bk=bank: E.tensor_tensor(
                ttileD[:, e_ * 512:(e_ + 1) * 512],
                tmine[:, e_ * 512:(e_ + 1) * 512], bk, OP.add),
                wait=["TT"], inc="VV")
        ln_apply(t3tile, ttileD, dst_wait=("GD",))
        b.dma("g", t3_sp[j], t3tile, "GD", wait=["SS"])
        if DEBUG:
            b.dma("g", dbg_t3[j * P:(j + 1) * P, :], t3tile, "GD")
    ar.pop()
    b.fence()

    ar.push()
    w1qT = ar.alloc(8 * I1, BF16)
    w2s = ar.alloc(4 * 512, BF16)
    b1hi = ar.alloc(I1, BF16)
    b1lo = ar.alloc(I1, BF16)
    b2hi = ar.alloc(D, BF16)
    b2lo = ar.alloc(D, BF16)
    ttileE = ar.alloc(D)
    xqE = ar.alloc(D, BF16)
    i32E = ar.alloc(I1, I32)
    xqTE = ar.alloc(8 * P, BF16)
    h1 = ar.alloc(I1)
    hqE = ar.alloc(I1, BF16)
    hqTE = ar.alloc(32 * P, BF16)
    y2 = ar.alloc(D)
    otile = ar.alloc(D)
    ccs = ar.alloc(2)
    cchi = ar.alloc(P, BF16)
    cclo = ar.alloc(P, BF16)
    ccrow_f = ar.alloc(P)
    ws127 = ar.alloc(2)
    b1f = h1
    b2f = y2

    for dc in range(8):
        b.dma("sy", w1qT[:, dc * I1:(dc + 1) * I1], w1q_sp[dc], "LD",
              wait=["GD"])
    b.dma("sy", b1f[0:1, :], b1_d, "LD")
    b.dma("sy", b2f[0:1, :], b2_d, "LD")
    b.op("s", lambda E: E.activation(b1hi[0:1, :], b1f[0:1, :], AF.Identity),
         wait=["LD"], inc="SS")
    b.op("v", lambda E: E.tensor_tensor(b1lo[0:1, :], b1f[0:1, :],
                                        b1hi[0:1, :], OP.subtract),
         wait=["SS"], inc="VV")
    b.op("s", lambda E: E.activation(b2hi[0:1, :], b2f[0:1, :], AF.Identity),
         inc="SS")
    b.op("v", lambda E: E.tensor_tensor(b2lo[0:1, :], b2f[0:1, :],
                                        b2hi[0:1, :], OP.subtract),
         wait=["SS"], inc="VV")
    b.op("v", lambda E: E.tensor_scalar(ws127[:, 0:1],
                                        gcol[:, W1SC:W1SC + 1],
                                        1.0 / 127.0, None, OP.mult),
         inc="VV")
    b.op("v", lambda E: E.tensor_scalar(ws127[:, 1:2],
                                        gcol[:, W2SC:W2SC + 1],
                                        1.0 / 127.0, None, OP.mult),
         inc="VV")

    def colc_rows(sccol):
        b.op("v", lambda E: E.reciprocal(ccs[:, 0:1], sccol), wait=["TT"],
             inc="VV")
        b.op("v", lambda E: E.memset(sq_scr[:, 0:P], 0.0), inc="VV")
        b.op("v", lambda E: E.tensor_copy(sq_scr[:, 0:1], ccs[:, 0:1]),
             inc="VV")
        b.op("t", lambda E: E.transpose(pt, sq_scr[:, 0:P], ident),
             wait=["VV"], inc="TT")
        b.op("v", lambda E: E.tensor_copy(ccrow_f[0:1, :], pt[0:1, :]),
             wait=["TT"], inc="VV")
        b.op("s", lambda E: E.activation(cchi[0:1, :], ccrow_f[0:1, :],
                                         AF.Identity), wait=["VV"], inc="SS")
        b.op("v", lambda E: E.tensor_tensor(cclo[0:1, :], ccrow_f[0:1, :],
                                            cchi[0:1, :], OP.subtract),
             wait=["SS"], inc="VV")

    for j in range(NT):
        if FENCE_D:
            b.fence()
        b.dma("sy", ttileE, t3_sp[j], "LD", wait=["TT", "VV", "SS"])
        b.op("v", lambda E: E.reduce_max(scr[:, 31:32], ttileE, axis=AX,
                                         apply_absolute_value=True),
             wait=["LD"], inc="VV")
        b.op("v", lambda E: E.tensor_scalar(scr[:, 31:32], scr[:, 31:32],
                                            1e-5, None, OP.max), inc="VV")
        b.op("v", lambda E: E.reciprocal(scr[:, 30:31], scr[:, 31:32]),
             inc="VV")
        b.op("v", lambda E: E.tensor_scalar(scr[:, 30:31], scr[:, 30:31],
                                            127.0, None, OP.mult), inc="VV")
        b.op("v", lambda E: E.tensor_scalar(sq_scr, ttileE, scr[:, 30:31],
                                            None, OP.mult), inc="VV")
        b.op("v", lambda E: E.tensor_copy(i32E[:, 0:D], sq_scr), inc="VV")
        b.op("v", lambda E: E.tensor_copy(xqE, i32E[:, 0:D]), inc="VV")
        for dc in range(8):
            b.op("t", lambda E, d=dc: E.transpose(
                ptb[:, d * P:(d + 1) * P], xqE[:, d * P:(d + 1) * P],
                identb), wait=["VV"], inc="TT")
        b.op("v", lambda E: E.tensor_copy(xqTE, ptb[:, 0:8 * P]),
             wait=["TT"], inc="VV")
        b.op("v", lambda E: E.tensor_tensor(ccs[:, 1:2], scr[:, 31:32],
                                            ws127[:, 0:1], OP.mult),
             inc="VV")
        colc_rows(ccs[:, 1:2])
        for hc in range(8):
            bank = pbk[hc % 2]
            for dc in range(8):
                b.op("t", lambda E, d=dc, h_=hc, bk=bank: E.matmul(
                    bk, xqTE[:, d * P:(d + 1) * P],
                    w1qT[:, d * I1 + h_ * 512:d * I1 + (h_ + 1) * 512],
                    start=(d == 0), stop=False), wait=["VV", "SS"],
                    inc="TT")
            b.op("t", lambda E, h_=hc, bk=bank: E.matmul(
                bk, cchi[0:1, :], b1hi[0:1, h_ * 512:(h_ + 1) * 512],
                start=False, stop=False), inc="TT")
            b.op("t", lambda E, h_=hc, bk=bank: E.matmul(
                bk, cclo[0:1, :], b1hi[0:1, h_ * 512:(h_ + 1) * 512],
                start=False, stop=False), inc="TT")
            b.op("t", lambda E, h_=hc, bk=bank: E.matmul(
                bk, cchi[0:1, :], b1lo[0:1, h_ * 512:(h_ + 1) * 512],
                start=False, stop=True), inc="TT")
            pe_flush()
            b.op("s", lambda E, h_=hc, bk=bank: E.activation(
                h1[:, h_ * 512:(h_ + 1) * 512], bk, AF.Gelu,
                scale=ccs[:, 1:2]), wait=["TT"], inc="SS")
        b.op("v", lambda E: E.reduce_max(scr[:, 29:30], h1, axis=AX,
                                         apply_absolute_value=True),
             wait=["SS"], inc="VV")
        b.op("v", lambda E: E.tensor_scalar(scr[:, 29:30], scr[:, 29:30],
                                            1e-5, None, OP.max), inc="VV")
        b.op("v", lambda E: E.reciprocal(scr[:, 28:29], scr[:, 29:30]),
             inc="VV")
        b.op("v", lambda E: E.tensor_scalar(scr[:, 28:29], scr[:, 28:29],
                                            127.0, None, OP.mult), inc="VV")
        b.op("v", lambda E: E.tensor_scalar(h1, h1, scr[:, 28:29], None,
                                            OP.mult), inc="VV")
        b.op("v", lambda E: E.tensor_copy(i32E, h1), inc="VV")
        b.op("v", lambda E: E.tensor_copy(hqE, i32E), inc="VV")
        for q8 in range(4):
            for k in range(8):
                blk = q8 * 8 + k
                b.op("t", lambda E, b_=blk, k_=k: E.transpose(
                    ptb[:, k_ * P:(k_ + 1) * P],
                    hqE[:, b_ * P:(b_ + 1) * P], identb),
                    wait=["VV"], inc="TT")
            b.op("v", lambda E, q_=q8: E.tensor_copy(
                hqTE[:, q_ * 8 * P:(q_ + 1) * 8 * P], ptb[:, 0:8 * P]),
                wait=["TT"], inc="VV")
        b.op("v", lambda E: E.tensor_tensor(ccs[:, 1:2], scr[:, 29:30],
                                            ws127[:, 1:2], OP.mult),
             inc="VV")
        colc_rows(ccs[:, 1:2])
        for ec in range(2):
            bank = pbk[2 + ec]
            for ic in range(32):
                wbuf = w2s[:, (ic % 2) * 1024:(ic % 2) * 1024 + 512]
                b.dma("sy", wbuf,
                      w2q_sp[ic][:, ec * 512:(ec + 1) * 512], "LD",
                      wait=["TT"])
                b.op("t", lambda E, i_=ic, bk=bank, wb=wbuf: E.matmul(
                    bk, hqTE[:, i_ * P:(i_ + 1) * P], wb,
                    start=(i_ == 0), stop=False), wait=["VV", "SS", "LD"],
                    inc="TT")
            b.op("t", lambda E, e_=ec, bk=bank: E.matmul(
                bk, cchi[0:1, :], b2hi[0:1, e_ * 512:(e_ + 1) * 512],
                start=False, stop=False), inc="TT")
            b.op("t", lambda E, e_=ec, bk=bank: E.matmul(
                bk, cclo[0:1, :], b2hi[0:1, e_ * 512:(e_ + 1) * 512],
                start=False, stop=False), inc="TT")
            b.op("t", lambda E, e_=ec, bk=bank: E.matmul(
                bk, cchi[0:1, :], b2lo[0:1, e_ * 512:(e_ + 1) * 512],
                start=False, stop=True), inc="TT")
            pe_flush()
            b.op("s", lambda E, e_=ec, bk=bank: E.activation(
                y2[:, e_ * 512:(e_ + 1) * 512], bk, AF.Identity,
                scale=ccs[:, 1:2]), wait=["TT"], inc="SS")
        b.op("v", lambda E: E.tensor_tensor(y2, y2, ttileE, OP.add),
             wait=["SS"], inc="VV")
        ln_apply(otile, y2, dst_wait=("GD",))
        b.dma("g", out_d[j * P:(j + 1) * P, :], otile, "GD", wait=["SS"])
    if DEBUG:
        b.dma("g", dbg_cxh, ctx1h_sp, "GD", wait=["GD"])
        b.dma("g", dbg_cxl, ctx1l_sp, "GD")
        b.dma("g", dbg_sc, gcol, "GD", wait=["VV", "SS"])
        b.dma("g", dbg_sc2, dbgc, "GD")
        b.dma("g", dbg_u[:, 0:16], nU0_0, "GD")
        b.dma("g", dbg_u[:, 16:32], nU0_1, "GD")
    b.op("g", lambda E: None, wait=["GD"])
    ar.pop()

    return b.finalize()


_PROG = None


def _host_consts(parity):
    bk = np.arange(P)[:, None]
    a = np.arange(P)[None, :]
    mleft = np.where(bk < a, np.float32(NEGBIG), np.float32(0.0)).astype(np.float32)
    mright = np.where(bk > a, np.float32(NEGBIG), np.float32(0.0)).astype(np.float32)
    fullneg = np.full((P, P), np.float32(NEGBIG), np.float32)
    zero = np.zeros((P, P), np.float32)
    ident = np.eye(P, dtype=np.float32)
    edgeL = fullneg if parity == 0 else mleft
    edgeR = mright if parity == 0 else fullneg
    m0 = mleft if parity == 0 else fullneg
    m1 = zero if parity == 0 else mleft
    dtab = np.zeros((P, 8 * 512), np.float32)
    for g in range(8):
        for m in range(4):
            dtab[:, g * 512 + m * P:g * 512 + (m + 1) * P] = (
                (4 * g + m) * P + bk - a - parity * P).astype(np.float32)
    sel = np.zeros((P, 2), np.float32)
    sel[:, parity] = 1.0
    return dict(ident=ident, mleft=mleft, mright=mright, edgeL=edgeL,
                edgeR=edgeR, m0=m0, m1=m1, dtab=dtab, sel=sel)


def kernel(**inputs):
    global _PROG
    if _PROG is None:
        _PROG = build_program()
    nc = _PROG
    Bn, T = 4, 4096
    x = np.ascontiguousarray(np.asarray(inputs["L0_attn_x"], np.float32))
    weights = {}
    for k in ("L0_attn_w_in", "L0_attn_w_out", "L0_ffn_w_gate",
              "L0_ffn_w_up", "L0_ffn_w_down", "L1_attn_W_qkv",
              "L1_attn_W_out", "L1_ffn_W1", "L1_ffn_W2"):
        weights[k] = np.ascontiguousarray(np.asarray(inputs[k], np.float32))
    for k, n in (("L0_attn_b_in", E3), ("L0_attn_b_out", D),
                 ("L1_ffn_b1", I1), ("L1_ffn_b2", D)):
        weights[k] = np.ascontiguousarray(
            np.asarray(inputs[k], np.float32).reshape(1, n))
    for k in ("L0_ffn_clip_x", "L0_ffn_clip_down"):
        weights[k] = np.ascontiguousarray(
            np.asarray(inputs[k], np.float32).reshape(1, 1))

    in_maps = []
    for c in range(8):
        bi, p = c // 2, c % 2
        xe = np.zeros((NTE * P, D), np.float32)
        lo = p * TOWN - P
        hi = p * TOWN + TOWN + P
        slo, shi = max(lo, 0), min(hi, T)
        xe[slo - lo:shi - lo] = x[bi, slo:shi]
        im = dict(weights)
        im["x_ext"] = xe
        im.update(_host_consts(p))
        in_maps.append(im)

    res = run_bass_kernel_spmd(nc, in_maps, core_ids=list(range(8)))
    global _LAST_RES
    _LAST_RES = res
    out = np.empty((Bn, T, D), np.float32)
    for c in range(8):
        bi, p = c // 2, c % 2
        oc = res.results[c]["out"]
        for j in range(NT):
            g = (2 * j + p) * P
            out[bi, g:g + P] = oc[j * P:(j + 1) * P]
    return out
